# version 01 (final)
# baseline (speedup 1.0000x reference)
import numpy as np
import ml_dtypes
from contextlib import ExitStack

import concourse.bass as bass
import concourse.mybir as mybir
from concourse.bass_utils import run_bass_kernel_spmd

F32 = mybir.dt.float32
BF16 = mybir.dt.bfloat16
I8 = mybir.dt.int8
I32 = mybir.dt.int32
AF = mybir.ActivationFunctionType
ALU = mybir.AluOpType
AX = mybir.AxisListType

NCORES = 8
HD = 128
NHL = 8
NS = 8
ISQD = float(1.0 / np.sqrt(HD))
POS_SCALE = float(np.sqrt(HD) * 0.3)
LNSQD = float(np.log(np.sqrt(HD)))
VTW = 132


class Sem:
    def __init__(self, ctx, nc, name):
        self.h = ctx.enter_context(nc.semaphore(name))
        self.n = 0


_DATA_OPS = frozenset([
    "tensor_copy", "tensor_tensor", "tensor_add", "tensor_sub", "tensor_mul",
    "tensor_max", "tensor_scalar", "tensor_single_scalar", "tensor_scalar_add",
    "tensor_scalar_sub", "tensor_scalar_mul", "tensor_scalar_max",
    "tensor_scalar_min", "tensor_relu", "tensor_reduce", "reduce_max",
    "reduce_sum", "scalar_tensor_tensor", "reciprocal", "memset", "bn_stats",
    "bn_aggr", "select", "activation", "copy", "mul", "add", "square", "sqrt",
    "sign", "memzero",
])


class TW:

    def __init__(self, eng):
        self._e = eng
        self._last = None

    def matmul(self, *a, **kw):
        self._last = self._e.matmul(*a, **kw)
        return self._last

    def transpose(self, *a, **kw):
        self._last = self._e.transpose(*a, **kw)
        return self._last

    def nop(self):
        r = self._last
        self._last = None
        if r is None:
            return self._e.nop()
        return r

    def __getattr__(self, name):
        return getattr(self._e, name)


class W:

    def __init__(self, eng, sem):
        self._e = eng
        self._s = sem

    def __getattr__(self, name):
        t = getattr(self._e, name)
        if name not in _DATA_OPS:
            return t
        sem = self._s
        eng = self._e

        def w(*a, **kw):
            r = t(*a, **kw)
            r.then_inc(sem.h, 1)
            sem.n += 1
            eng.wait_ge(sem.h, sem.n)
            return _Tick(eng)
        return w


class _Tick:

    def __init__(self, eng):
        self._e = eng

    def then_inc(self, sem, k):
        return self._e.nop().then_inc(sem, k)


def build_program(cfg):
    S, DM, ITR = cfg["S"], cfg["DM"], cfg["ITR"]
    dumps = cfg.get("dumps", ())
    stop_after = cfg.get("stop_after", 99)
    R = S // 8
    PR = S // 4
    LNR = S // 2
    NCC = DM // 128
    NSC = S // 512
    NST = S // 128
    NQC = PR // 1024
    NQT = PR // 128
    NQB = PR // 512
    NOT_ = R // 128
    NKT = S // 128
    NJ = NKT // 2
    NIT = ITR // 128
    NIM = ITR // 512
    NCT = DM // 128
    NC5 = DM // 512
    T1 = LNR // 128
    HW = NHL * HD
    inv_dm = 1.0 / DM

    nc = bass.Bass("TRN2", target_bir_lowering=False)

    d_xln = nc.dram_tensor("x_ln", [LNR, DM], F32, kind="ExternalInput")
    d_wq = nc.dram_tensor("wq", [DM, HW], F32, kind="ExternalInput")
    d_wk = nc.dram_tensor("wk", [DM, HW], F32, kind="ExternalInput")
    d_wv = nc.dram_tensor("wv", [DM, HW], F32, kind="ExternalInput")
    d_wo = nc.dram_tensor("wo", [DM, DM], F32, kind="ExternalInput")
    d_pos = nc.dram_tensor("spl_pos", [NHL * NS, HD], F32, kind="ExternalInput")
    d_dir = nc.dram_tensor("spl_dir", [NHL * NS, HD], F32, kind="ExternalInput")
    d_ls64 = nc.dram_tensor("spl_ls64", [NHL * NS, 1], F32, kind="ExternalInput")
    d_la = nc.dram_tensor("spl_la", [NHL, NS], F32, kind="ExternalInput")
    d_scal = nc.dram_tensor("scal", [1, 2], F32, kind="ExternalInput")
    d_wg = nc.dram_tensor("wgate", [DM, ITR], I8, kind="ExternalInput")
    d_wu = nc.dram_tensor("wup", [DM, ITR], I8, kind="ExternalInput")
    d_wd = nc.dram_tensor("wdown", [ITR, DM], I8, kind="ExternalInput")
    d_sg = nc.dram_tensor("sgate", [128, NIT], F32, kind="ExternalInput")
    d_su = nc.dram_tensor("sup", [128, NIT], F32, kind="ExternalInput")
    d_sd = nc.dram_tensor("sdown", [128, NCT], F32, kind="ExternalInput")
    d_id = nc.dram_tensor("ident_b", [128, 128], BF16, kind="ExternalInput")
    d_idf = nc.dram_tensor("ident_f", [128, 128], F32, kind="ExternalInput")
    d_onesb = nc.dram_tensor("ones_b", [128, 128], BF16, kind="ExternalInput")
    d_onesf = nc.dram_tensor("ones_f", [1, 128], F32, kind="ExternalInput")
    d_magic = nc.dram_tensor("magic", [1, 4], I32, kind="ExternalInput")
    d_idx = nc.dram_tensor("idx", [1, 8], I32, kind="ExternalInput")
    d_fcol = nc.dram_tensor("fcol", [128, 2], F32, kind="ExternalInput")
    d_xown = nc.dram_tensor("x_own", [R, DM], F32, kind="ExternalInput")

    d_out = nc.dram_tensor("out", [R, DM], F32, kind="ExternalOutput")

    dr_kt = nc.dram_tensor("dr_kt", [NHL, 128, S], BF16, kind="Internal")
    dr_v = nc.dram_tensor("dr_v", [2, S, 4 * HD], BF16, kind="Internal")
    dr_pq = nc.dram_tensor("dr_pq", [PR, DM], BF16, kind="Internal")
    dr_lxn = nc.dram_tensor("dr_lxn", [LNR, DM], BF16, kind="Internal")
    dr_ctx = nc.dram_tensor("dr_ctx", [NQT, 128, HW], BF16, kind="Internal")
    dr_sib = nc.dram_tensor("dr_sib", [R, HW], BF16, kind="Internal")
    dr_sc = nc.dram_tensor("dr_sc", [NHL * NS, 1], F32, kind="Internal")

    shsp = "Local" if cfg.get("local_shared") else "Shared"
    sh_xn = nc.dram_tensor("sh_xn", [S, DM], BF16, kind="Internal",
                           addr_space=shsp)
    sh_ctx = nc.dram_tensor("sh_ctx", [2, R, HW], BF16, kind="Internal",
                            addr_space=shsp)
    sh_flg = nc.dram_tensor("sh_flg", [2, 8], I32, kind="Internal",
                            addr_space=shsp)

    DH = {}
    if "xn" in dumps:
        DH["xn"] = nc.dram_tensor("dump_xn", [S, DM], BF16, kind="ExternalOutput")
    if "ktv" in dumps:
        DH["kt"] = nc.dram_tensor("dump_kt", [NHL, 128, S], BF16,
                                  kind="ExternalOutput")
        DH["v"] = nc.dram_tensor("dump_v", [2, S, 4 * HD], BF16,
                                 kind="ExternalOutput")
        DH["ka"] = nc.dram_tensor("dump_ka", [128, NST * 64], BF16,
                                  kind="ExternalOutput")
    if "qt" in dumps:
        DH["qt"] = nc.dram_tensor("dump_qt", [128, NHL * PR], BF16,
                                  kind="ExternalOutput")
        DH["at"] = nc.dram_tensor("dump_at", [8, NHL * PR], BF16,
                                  kind="ExternalOutput")
        DH["ncb"] = nc.dram_tensor("dump_ncb", [128, 8], F32,
                                   kind="ExternalOutput")
    if "ctx" in dumps:
        DH["ctx"] = nc.dram_tensor("dump_ctx", [NQT, 128, HW], BF16,
                                   kind="ExternalOutput")
    if "xch" in dumps:
        DH["xch"] = nc.dram_tensor("dump_xch", [128, 2 * DM], BF16,
                                   kind="ExternalOutput")
    if "s" in dumps:
        DH["s"] = nc.dram_tensor("dump_s", [128, NOT_ * DM], BF16,
                                 kind="ExternalOutput")

    top = ExitStack()
    with top:
        def sb(stk, name, shape, dt):
            return stk.enter_context(nc.sbuf_tensor(name, shape, dt))

        XNT_OWN = sb(top, "XNT_OWN", [128, NCC * R], BF16)
        IDENT = sb(top, "IDENT", [128, 128], BF16)
        IDENTF = sb(top, "IDENTF", [128, 128], F32)
        ONESB = sb(top, "ONESB", [128, 128], BF16)
        ONESF = sb(top, "ONESF", [1, 128], F32)
        POS_T = sb(top, "POS_T", [128, 64], BF16)
        DIR_T = sb(top, "DIR_T", [128, 64], BF16)
        POSSQ = sb(top, "POSSQ", [128, 64], F32)
        PDRr = sb(top, "PDRr", [128, 64], F32)
        NINVR = sb(top, "NINVR", [128, 64], F32)
        AMPR = sb(top, "AMPR", [128, 64], F32)
        DS4 = sb(top, "DS4", [128, 4], F32)
        OMC = sb(top, "OMC", [128, 2], F32)
        NCBT = sb(top, "NCBT", [128, 8], F32)
        MXK = sb(top, "MXK", [128, 8], F32)
        MXQ = sb(top, "MXQ", [128, 8], F32)
        SGC = sb(top, "SGC", [128, NIT], F32)
        SUC = sb(top, "SUC", [128, NIT], F32)
        SDCc = sb(top, "SDCc", [128, NCT], F32)
        MGT = sb(top, "MGT", [1, 4], I32)
        FLGS = sb(top, "FLGS", [1, 8], I32)
        IDXT = sb(top, "IDXT", [1, 8], I32)
        FCN = sb(top, "FCN", [128, 2], F32)

        SEMN = ["s%02d" % i for i in range(26)]
        SM = [Sem(top, nc, nm) for nm in SEMN]
        SL = [Sem(top, nc, "sl%02d" % i) for i in range(18)]
        GL = [Sem(top, nc, "gl%02d" % i) for i in range(12)]
        mono = nc.monotonic_semaphore(0, engine=nc.gpsimd)

        _bar_uid = [0]

        def barrier(g, w0, waits, flag_sem):
            for sh, cn in waits:
                g.wait_ge(sh, cn)
            if cfg.get("no_barrier"):
                return
            _bar_uid[0] += 1
            u = "b%d" % _bar_uid[0]
            payload_sem = flag_sem
            with g.register("pr" + u) as pr, g.register("sib" + u) as sib:
                g.reg_load(pr, IDXT[0:1, 0:1])
                g.reg_load(sib, IDXT[0:1, 1:2])
                prv = g.snap(pr, min_val=0, max_val=1)
                g.dma_start(out=sh_flg[bass.ds(prv, 1), w0:w0 + 2],
                            in_=MGT[0:1, w0:w0 + 2]
                            ).then_inc(payload_sem.h, 16)
                payload_sem.n += 16
                g.wait_ge(payload_sem.h, payload_sem.n)
                sibv = g.snap(sib, min_val=0, max_val=1)
                with (g.register("nd" + u) as nd, g.register("f0" + u) as f0,
                      g.register("f1" + u) as f1, g.register("t0" + u) as t0,
                      g.register("t1" + u) as t1, g.register("m0" + u) as m0,
                      g.register("m1" + u) as m1):
                    g.reg_load(m0, MGT[0:1, w0:w0 + 1])
                    g.reg_load(m1, MGT[0:1, w0 + 1:w0 + 2])
                    g.reg_mov(nd, 1)
                    with g.While(nd):
                        mono.inc_expected(16)
                        g.dma_start(out=FLGS[0:1, w0:w0 + 2],
                                    in_=sh_flg[bass.ds(sibv, 1), w0:w0 + 2]
                                    ).then_inc(mono.sem(), 16)
                        mono.wait()
                        g.reg_load(f0, FLGS[0:1, w0:w0 + 1])
                        g.reg_load(f1, FLGS[0:1, w0 + 1:w0 + 2])
                        g.reg_alu(t0, f0, m0, ALU.not_equal)
                        g.reg_alu(t1, f1, m1, ALU.not_equal)
                        g.reg_alu(nd, t0, t1, ALU.logical_or)

        LD, P0 = SM[0], SM[1]
        with (nc.sbuf_tensor("W64A", [64, 128], F32) as W64A,
              nc.sbuf_tensor("W64B", [64, 128], F32) as W64B,
              nc.sbuf_tensor("W64S", [64, 128], F32) as W64S,
              nc.sbuf_tensor("W64AB", [64, 128], BF16) as W64AB,
              nc.sbuf_tensor("W64BB", [64, 128], BF16) as W64BB,
              nc.sbuf_tensor("C64", [64, 8], F32) as C64,
              nc.sbuf_tensor("A88", [8, 20], F32) as A88,
              nc.sbuf_tensor("R14", [1, 8], F32) as R14,
              nc.sbuf_tensor("ROW1", [1, 64], F32) as ROW1,
              nc.psum_tensor("PTB", [128, 64], BF16) as PTB,
              nc.psum_tensor("PTF", [1, 64], F32) as PTF,
              nc.psum_tensor("PBC", [128, 64], F32) as PBC,
              nc.Block() as blk):

            @blk.sync
            def _(sy):
                for dst, src in [(IDENT[:, :], d_id[:, :]),
                                 (IDENTF[:, :], d_idf[:, :]),
                                 (ONESB[:, :], d_onesb[:, :]),
                                 (ONESF[:, :], d_onesf[:, :]),
                                 (W64A[:, :], d_pos[:, :]),
                                 (W64B[:, :], d_dir[:, :]),
                                 (C64[:, 0:1], d_ls64[:, :]),
                                 (A88[0:8, 0:8], d_la[:, :]),
                                 (R14[0:1, 0:2], d_scal[:, :]),
                                 (MGT[:, :], d_magic[:, :]),
                                 (IDXT[:, :], d_idx[:, :]),
                                 (FCN[:, :], d_fcol[:, :]),
                                 (SGC[:, :], d_sg[:, :]),
                                 (SUC[:, :], d_su[:, :]),
                                 (SDCc[:, :], d_sd[:, :])]:
                    sy.dma_start(out=dst, in_=src).then_inc(LD.h, 16)
                LD.n += 16 * 15
                sy.wait_ge(LD.h, LD.n)

            @blk.scalar
            def _(sc):
                sc = W(sc, SM[21])
                sc.wait_ge(LD.h, 16 * 15)
                sc.activation(R14[0:1, 2:3], R14[0:1, 0:1], AF.Sigmoid)
                sc.activation(R14[0:1, 3:4], R14[0:1, 1:2], AF.Sigmoid)
                sc.nop().then_inc(P0.h, 1)
                sc.wait_ge(P0.h, 2)
                sc.activation(C64[:, 3:4], C64[:, 1:2], AF.Ln)
                sc.activation(C64[:, 3:4], C64[:, 3:4], AF.Exp, scale=0.5)
                sc.activation(C64[:, 4:5], C64[:, 2:3], AF.Ln)
                sc.activation(C64[:, 4:5], C64[:, 4:5], AF.Exp, scale=0.5)
                sc.nop().then_inc(P0.h, 1)
                sc.wait_ge(P0.h, 4)
                sc.activation(C64[:, 3:4], C64[:, 0:1], AF.Exp)
                sc.activation(A88[0:8, 8:16], A88[0:8, 0:8], AF.Exp,
                              bias=A88[0:8, 16:17], accum_out=A88[0:8, 17:18])
                sc.nop().then_inc(P0.h, 1)

            @blk.vector
            def _(v):
                v = W(v, SM[20])
                v.wait_ge(LD.h, 16 * 15)
                v.memset(R14[0:1, 4:8], 0.0)
                v.tensor_mul(W64S[:, :], W64A[:, :], W64A[:, :])
                v.reduce_sum(C64[:, 1:2], W64S[:, :], axis=AX.X)
                v.tensor_mul(W64S[:, :], W64B[:, :], W64B[:, :])
                v.reduce_sum(C64[:, 2:3], W64S[:, :], axis=AX.X)
                v.reduce_max(A88[0:8, 16:17], A88[0:8, 0:8], axis=AX.X)
                v.tensor_scalar_mul(A88[0:8, 16:17], A88[0:8, 16:17], -1.0)
                v.nop().then_inc(P0.h, 1)
                v.wait_ge(P0.h, 3)
                v.tensor_scalar_add(C64[:, 3:4], C64[:, 3:4], 1e-12)
                v.reciprocal(C64[:, 5:6], C64[:, 3:4])
                v.tensor_scalar_mul(C64[:, 5:6], C64[:, 5:6], POS_SCALE)
                v.tensor_scalar_add(C64[:, 4:5], C64[:, 4:5], 1e-12)
                v.reciprocal(C64[:, 6:7], C64[:, 4:5])
                v.tensor_scalar(W64A[:, :], W64A[:, :], C64[:, 5:6], None,
                                ALU.mult)
                v.tensor_scalar(W64B[:, :], W64B[:, :], C64[:, 6:7], None,
                                ALU.mult)
                v.tensor_mul(W64S[:, :], W64A[:, :], W64A[:, :])
                v.reduce_sum(C64[:, 1:2], W64S[:, :], axis=AX.X)
                v.tensor_mul(W64S[:, :], W64A[:, :], W64B[:, :])
                v.reduce_sum(C64[:, 2:3], W64S[:, :], axis=AX.X)
                v.tensor_copy(W64AB[:, :], W64A[:, :])
                v.tensor_copy(W64BB[:, :], W64B[:, :])
                v.nop().then_inc(P0.h, 1)
                v.wait_ge(P0.h, 5)
                v.tensor_scalar_min(C64[:, 3:4], C64[:, 3:4], 1.2)
                v.tensor_scalar_max(C64[:, 3:4], C64[:, 3:4], 0.3)
                v.tensor_mul(C64[:, 4:5], C64[:, 3:4], C64[:, 3:4])
                v.reciprocal(C64[:, 4:5], C64[:, 4:5])
                v.tensor_scalar_mul(C64[:, 3:4], C64[:, 4:5], -0.5)
                v.reciprocal(A88[0:8, 17:18], A88[0:8, 17:18])
                v.tensor_scalar(A88[0:8, 8:16], A88[0:8, 8:16],
                                A88[0:8, 17:18], None, ALU.mult)
                v.tensor_scalar_mul(R14[0:1, 3:4], R14[0:1, 3:4], 0.1)
                v.tensor_scalar_min(R14[0:1, 3:4], R14[0:1, 3:4], 0.05)
                v.nop().then_inc(P0.h, 1)
                v.wait_ge(P0.h, 8)
                v.tensor_copy(POS_T[:, :], PTB[0:128, 0:64])
                v.nop().then_inc(P0.h, 1)
                v.wait_ge(P0.h, 10)
                v.tensor_copy(DIR_T[:, :], PTB[0:128, 0:64])
                v.nop().then_inc(P0.h, 1)
                for irow in range(4):
                    dst = [POSSQ, PDRr, NINVR, AMPR][irow]
                    v.wait_ge(P0.h, 12 + 4 * irow)
                    v.tensor_copy(ROW1[0:1, :], PTF[0:1, 0:64])
                    v.nop().then_inc(P0.h, 1)
                    v.wait_ge(P0.h, 14 + 4 * irow)
                    v.tensor_copy(dst[:, :], PBC[:, 0:64])
                    v.nop().then_inc(P0.h, 1)
                v.wait_ge(P0.h, 28)
                v.tensor_copy(DS4[:, :], PBC[:, 0:4])
                v.tensor_scalar(OMC[:, 0:1], DS4[:, 0:1], -1.0, 1.0,
                                ALU.mult, ALU.add)
                v.tensor_scalar(OMC[:, 1:2], DS4[:, 1:2], -1.0, 1.0,
                                ALU.mult, ALU.add)
                v.nop().then_inc(P0.h, 1)

            @blk.gpsimd
            def _(g):
                GP1 = SM[22]
                g.wait_ge(P0.h, 6)
                g.dma_start(out=dr_sc.ap().rearrange("(h n) o -> h (n o)",
                                                     h=8),
                            in_=A88[0:8, 8:16]).then_inc(GP1.h, 16)
                GP1.n += 16
                g.wait_ge(GP1.h, GP1.n)
                g.dma_start(out=C64[:, 7:8], in_=dr_sc[:, :]
                            ).then_inc(GP1.h, 16)
                GP1.n += 16
                g.wait_ge(GP1.h, GP1.n)
                g.nop().then_inc(P0.h, 1)

            @blk.tensor
            def _(t):
                t = TW(t)
                t.wait_ge(P0.h, 7)
                t.transpose(PTB[0:128, 0:64], W64AB[:, :], IDENT[0:64, 0:64])
                t.nop().then_inc(P0.h, 1)
                t.wait_ge(P0.h, 9)
                t.transpose(PTB[0:128, 0:64], W64BB[:, :], IDENT[0:64, 0:64])
                t.nop().then_inc(P0.h, 1)
                for irow in range(4):
                    colj = [1, 2, 3, 7][irow]
                    t.wait_ge(P0.h, 11 + 4 * irow)
                    t.transpose(PTF[0:1, 0:64], C64[:, colj:colj + 1],
                                IDENTF[0:64, 0:64])
                    t.nop().then_inc(P0.h, 1)
                    t.wait_ge(P0.h, 13 + 4 * irow)
                    t.matmul(PBC[:, 0:64], ONESF[0:1, :], ROW1[0:1, :],
                             start=True, stop=True)
                    t.nop().then_inc(P0.h, 1)
                t.wait_ge(P0.h, 27)
                t.matmul(PBC[:, 0:4], ONESF[0:1, :], R14[0:1, 2:6],
                         start=True, stop=True)
                t.nop().then_inc(P0.h, 1)
            P0.n = 29

        if stop_after < 1:
            return nc, DH
        PUB, XSm, ACm, EVm = SM[2], SM[3], SM[4], SM[5]
        with (nc.sbuf_tensor("XF", [128, 4 * DM], F32) as XF,
              nc.sbuf_tensor("XNB", [128, 2 * DM], BF16) as XNB,
              nc.sbuf_tensor("SQS", [128, DM], F32) as SQS,
              nc.sbuf_tensor("LNC", [128, 16], F32) as LNC,
              nc.Block() as blk):

            def xf_cnt(t):
                return 16 * (t // 4 + 1)

            def pub_cnt(t):
                return 16 * (t // 2 + 1)

            @blk.sync
            def _(sy):
                for t in range(T1):
                    if t >= 4:
                        sy.wait_ge(ACm.h, 3 * (t - 4) + 3)
                    sy.dma_start(out=XF[:, (t % 4) * DM:(t % 4 + 1) * DM],
                                 in_=d_xln[t * 128:(t + 1) * 128, :]
                                 ).then_inc(SL[t % 4].h, 16)
            for t in range(T1):
                SL[t % 4].n += 16

            @blk.scalar
            def _(sc):
                sc = W(sc, SM[21])
                for t in range(T1):
                    b = t % 4
                    c = b * 4
                    sc.wait_ge(SL[b].h, xf_cnt(t))
                    sc.activation(SQS[:, :], XF[:, b * DM:(b + 1) * DM],
                                  AF.Square, accum_out=LNC[:, c + 1:c + 2]
                                  ).then_inc(ACm.h, 1)
                    sc.wait_ge(EVm.h, 2 * t + 1)
                    sc.activation(LNC[:, c + 2:c + 3], LNC[:, c + 2:c + 3],
                                  AF.Ln, bias=FCN[:, 0:1])
                    sc.activation(LNC[:, c + 2:c + 3], LNC[:, c + 2:c + 3],
                                  AF.Exp, scale=-0.5).then_inc(ACm.h, 1)
                    sc.wait_ge(EVm.h, 2 * t + 2)
                    if t >= 2:
                        sc.wait_ge(GL[t % 2].h, pub_cnt(t - 2))
                    sc.activation(XNB[:, (t % 2) * DM:(t % 2 + 1) * DM],
                                  XF[:, b * DM:(b + 1) * DM], AF.Identity,
                                  bias=LNC[:, c + 3:c + 4],
                                  scale=LNC[:, c + 2:c + 3]).then_inc(ACm.h, 1)
            ACm.n += 3 * T1

            @blk.vector
            def _(v):
                v = W(v, SM[20])
                for t in range(T1):
                    b = t % 4
                    c = b * 4
                    v.wait_ge(SL[b].h, xf_cnt(t))
                    v.reduce_sum(LNC[:, c:c + 1], XF[:, b * DM:(b + 1) * DM],
                                 axis=AX.X)
                    v.tensor_scalar_mul(LNC[:, c:c + 1], LNC[:, c:c + 1],
                                        inv_dm)
                    v.wait_ge(ACm.h, 3 * t + 1)
                    v.tensor_mul(LNC[:, c + 2:c + 3], LNC[:, c:c + 1],
                                 LNC[:, c:c + 1])
                    v.tensor_scalar(LNC[:, c + 1:c + 2], LNC[:, c + 1:c + 2],
                                    inv_dm, None, ALU.mult)
                    v.tensor_sub(LNC[:, c + 2:c + 3], LNC[:, c + 1:c + 2],
                                 LNC[:, c + 2:c + 3]).then_inc(EVm.h, 1)
                    v.wait_ge(ACm.h, 3 * t + 2)
                    v.tensor_mul(LNC[:, c + 3:c + 4], LNC[:, c:c + 1],
                                 LNC[:, c + 2:c + 3])
                    v.tensor_scalar_mul(LNC[:, c + 3:c + 4],
                                        LNC[:, c + 3:c + 4], -1.0
                                        ).then_inc(EVm.h, 1)
            EVm.n += 2 * T1

            @blk.gpsimd
            def _(g):
                for t in range(T1):
                    g.wait_ge(ACm.h, 3 * t + 3)
                    g.dma_start(out=dr_lxn[t * 128:(t + 1) * 128, :],
                                in_=XNB[:, (t % 2) * DM:(t % 2 + 1) * DM]
                                ).then_inc(GL[t % 2].h, 16)
                for t in range(T1):
                    GL[t % 2].n += 16
                g.wait_ge(GL[0].h, GL[0].n)
                g.wait_ge(GL[1].h, GL[1].n)
                with g.register("rb") as rb:
                    g.reg_load(rb, IDXT[0:1, 4:5])
                    rv = g.snap(rb, donate=True, min_val=0, max_val=S - LNR)
                    g.dma_start(out=sh_xn[bass.ds(rv, LNR), :],
                                in_=dr_lxn[:, :]).then_inc(GL[0].h, 16)
                GL[0].n += 16
                barrier(g, 0, [(GL[0].h, GL[0].n)], PUB)

        if "xn" in DH:
            with nc.Block() as blk:
                @blk.sync
                def _(sy):
                    sy.dma_start(out=DH["xn"].ap(), in_=sh_xn.ap()
                                 ).then_inc(PUB.h, 16)
                    PUB.n += 16
                    sy.wait_ge(PUB.h, PUB.n)

        if stop_after < 2:
            return nc, DH
        attn1 = ExitStack()
        KA = sb(attn1, "KA", [128, NST * 64], BF16)

        XL, WL, KTM, VM, KAM = SM[3], SM[22], SM[6], SM[7], SM[8]
        EVK, EVV, KSQm, WBK, WBV = SM[9], SM[10], SM[11], SM[12], SM[13]
        KA1, KA2, KAE = SM[14], SM[15], SM[16]
        CH = 512
        with (nc.sbuf_tensor("XSB", [128, 2 * NCC * CH], BF16) as XSB,
              nc.sbuf_tensor("WKV", [128, NCC * 2048], BF16) as WKV,
              nc.sbuf_tensor("KTE", [128, NHL * CH], BF16) as KTE,
              nc.sbuf_tensor("KSQ", [128, NHL * CH], BF16) as KSQ,
              nc.sbuf_tensor("VE", [128, 2 * 512], BF16) as VE,
              nc.sbuf_tensor("TS64", [128, 64], F32) as TS64,
              nc.sbuf_tensor("KSC", [128, 4 * 64], F32) as KSC,
              nc.psum_tensor("PKT", [128, 2 * CH], F32) as PKT,
              nc.psum_tensor("PV5", [128, 2 * 512], F32) as PV5,
              nc.psum_tensor("PKA", [128, 128], F32) as PKA,
              nc.psum_tensor("PTS", [128, 64], F32) as PTS,
              nc.Block() as blk):
            xl0, wl0 = XL.n, WL.n
            xsl0 = [SL[6].n, SL[7].n]
            wbk_sl0 = [GL[2].n, GL[3].n]
            wbv_sl0 = [GL[4].n, GL[5].n]
            ktm0, vm0, kam0 = KTM.n, VM.n, KAM.n
            evk0, evv0, ksq0 = EVK.n, EVV.n, KSQm.n
            wbk0, wbv0 = WBK.n, WBV.n
            ka10, ka20, kae0 = KA1.n, KA2.n, KAE.n
            NVT = CH // 128

            @blk.gpsimd
            def _(g):
                for cj in range(NCC):
                    g.dma_start(out=WKV[:, cj * 2048:cj * 2048 + HW],
                                in_=d_wk[cj * 128:(cj + 1) * 128, :]
                                ).then_inc(WL.h, 16)
                    g.dma_start(out=WKV[:, cj * 2048 + 1024:(cj + 1) * 2048],
                                in_=d_wv[cj * 128:(cj + 1) * 128, :]
                                ).then_inc(WL.h, 16)
                WL.n = wl0 + 32 * NCC
                for sc in range(NSC):
                    for h in range(NHL):
                        g.wait_ge(EVK.h, evk0 + sc * NHL + h + 1)
                        g.dma_start(out=dr_kt[h, :, sc * CH:(sc + 1) * CH],
                                    in_=KTE[:, h * CH:(h + 1) * CH]
                                    ).then_inc(GL[2 + sc % 2].h, 16)
                    for gs in range(2 * NVT):
                        gu = sc * 2 * NVT + gs
                        g.wait_ge(EVV.h, evv0 + gu + 1)
                        grp, st = divmod(gs, NVT)
                        g.dma_start(
                            out=dr_v[grp, sc * CH + st * 128:
                                     sc * CH + (st + 1) * 128, :],
                            in_=VE[:, (gu % 2) * 512:(gu % 2 + 1) * 512]
                            ).then_inc(GL[4 + gu % 2].h, 16)
                for q, tot in [(2, 16 * NHL * ((NSC + 1) // 2)),
                               (3, 16 * NHL * (NSC // 2)),
                               (4, 16 * NVT * NSC), (5, 16 * NVT * NSC)]:
                    g.wait_ge(GL[q].h, GL[q].n + tot)
            for sc in range(NSC):
                GL[2 + sc % 2].n += 16 * NHL
                GL[4 + (sc * 2 * NVT) % 2].n += 16 * NVT
                GL[4 + (sc * 2 * NVT + 1) % 2].n += 16 * NVT

            @blk.sync
            def _(sy):
                for sc in range(NSC):
                    b = sc % 2
                    if sc >= 2:
                        sy.wait_ge(KAM.h, kam0 + (sc - 1) * NVT)
                    for cj in range(NCC):
                        sy.dma_start(
                            out=XSB[:, (b * NCC + cj) * CH:
                                    (b * NCC + cj + 1) * CH],
                            in_=sh_xn[sc * CH:(sc + 1) * CH,
                                      cj * 128:(cj + 1) * 128],
                            transpose=True).then_inc(SL[6 + b].h, 16)
                if "kt" in DH:
                    sy.wait_ge(KAM.h, kam0 + NSC * NVT)
                    sy.wait_ge(KAE.h, kae0 + NSC * NVT)
                    for q in range(2, 6):
                        sy.wait_ge(GL[q].h, GL[q].n)
                    for dst, src in [(DH["kt"].ap(), dr_kt.ap()),
                                     (DH["v"].ap(), dr_v.ap()),
                                     (DH["ka"][:, :], KA[:, :])]:
                        sy.dma_start(out=dst, in_=src).then_inc(SL[0].h, 16)
                        SL[0].n += 16
                    sy.wait_ge(SL[0].h, SL[0].n)
            for sc in range(NSC):
                SL[6 + sc % 2].n += 16 * NCC

            @blk.tensor
            def _(t):
                t = TW(t)
                t.wait_ge(WL.h, wl0 + 32 * NCC)
                for sc in range(NSC):
                    b = sc % 2
                    t.wait_ge(SL[6 + b].h, xsl0[b] + 16 * NCC * (sc // 2 + 1))
                    xb = b * NCC * CH
                    for h in range(NHL):
                        gi = sc * NHL + h
                        if gi >= 2:
                            t.wait_ge(EVK.h, evk0 + gi - 1)
                        pb = (h % 2) * CH
                        for cj in range(NCC):
                            t.matmul(PKT[:, pb:pb + CH],
                                     WKV[:, cj * 2048 + h * 128:
                                         cj * 2048 + (h + 1) * 128],
                                     XSB[:, xb + cj * CH:xb + (cj + 1) * CH],
                                     start=(cj == 0), stop=(cj == NCC - 1))
                        t.nop().then_inc(KTM.h, 1)
                    for gs in range(2 * NVT):
                        grp, st = divmod(gs, NVT)
                        gu = sc * 2 * NVT + gs
                        if gu >= 2:
                            t.wait_ge(EVV.h, evv0 + gu - 1)
                        vb = (gs % 2) * 512
                        for cj in range(NCC):
                            t.matmul(PV5[:, vb:vb + 512],
                                     XSB[:, xb + cj * CH + st * 128:
                                         xb + cj * CH + (st + 1) * 128],
                                     WKV[:, cj * 2048 + 1024 + grp * 512:
                                         cj * 2048 + 1024 + (grp + 1) * 512],
                                     start=(cj == 0), stop=(cj == NCC - 1))
                        t.nop().then_inc(VM.h, 1)
                    t.wait_ge(KSQm.h, ksq0 + (sc + 1) * NHL)
                    for st in range(NVT):
                        gk = sc * NVT + st
                        if gk >= 1:
                            t.wait_ge(KAE.h, kae0 + gk)
                        for h in range(NHL):
                            t.matmul(PKA[:, h * 16:h * 16 + 8],
                                     KTE[:, h * CH + st * 128:
                                         h * CH + (st + 1) * 128],
                                     POS_T[:, h * 8:(h + 1) * 8],
                                     start=True, stop=True)
                            t.matmul(PKA[:, h * 16 + 8:(h + 1) * 16],
                                     KTE[:, h * CH + st * 128:
                                         h * CH + (st + 1) * 128],
                                     DIR_T[:, h * 8:(h + 1) * 8],
                                     start=True, stop=True)
                            t.matmul(PTS[:, h * 8:(h + 1) * 8],
                                     KSQ[:, h * CH + st * 128:
                                         h * CH + (st + 1) * 128],
                                     ONESB[:, 0:8],
                                     start=True, stop=True)
                        t.nop().then_inc(KAM.h, 1)
            KTM.n = ktm0 + NSC * NHL
            VM.n = vm0 + NSC * 2 * NVT
            KAM.n = kam0 + NSC * NVT

            @blk.vector
            def _(v):
                v = W(v, SM[20])
                for sc in range(NSC):
                    for h in range(NHL):
                        gi = sc * NHL + h
                        v.wait_ge(KTM.h, ktm0 + gi + 1)
                        if sc >= 1:
                            if h == 0:
                                v.wait_ge(KAM.h, kam0 + sc * NVT)
                                v.wait_ge(GL[2 + (sc - 1) % 2].h,
                                          wbk_sl0[(sc - 1) % 2] +
                                          16 * NHL * ((sc - 1) // 2 + 1))
                        pb = (h % 2) * CH
                        v.tensor_copy(KTE[:, h * CH:(h + 1) * CH],
                                      PKT[:, pb:pb + CH]).then_inc(EVK.h, 1)
                        v.tensor_mul(KSQ[:, h * CH:(h + 1) * CH],
                                     KTE[:, h * CH:(h + 1) * CH],
                                     KTE[:, h * CH:(h + 1) * CH]
                                     ).then_inc(KSQm.h, 1)
                    for gs in range(2 * NVT):
                        gu = sc * 2 * NVT + gs
                        v.wait_ge(VM.h, vm0 + gu + 1)
                        if gu >= 2:
                            v.wait_ge(GL[4 + gu % 2].h,
                                      wbv_sl0[gu % 2] + 16 * (gu // 2))
                        v.tensor_copy(VE[:, (gu % 2) * 512:(gu % 2 + 1) * 512],
                                      PV5[:, (gu % 2) * 512:(gu % 2 + 1) * 512]
                                      ).then_inc(EVV.h, 1)
                    for st in range(NVT):
                        gk = sc * NVT + st
                        v.wait_ge(KAM.h, kam0 + gk + 1)
                        tpv = PKA[:, :].rearrange("p (h w n) -> p h w n",
                                                  w=2, n=8)
                        v.tensor_copy(TS64[:, :], PTS[:, :])
                        if gk == 0:
                            v.tensor_copy(
                                MXK[:, :].rearrange("p (h o) -> p h o", o=1),
                                PTS[:, :].rearrange("p (h n) -> p h n",
                                                    n=8)[:, :, 0:1])
                        else:
                            v.tensor_tensor(
                                MXK[:, :].rearrange("p (h o) -> p h o", o=1),
                                MXK[:, :].rearrange("p (h o) -> p h o", o=1),
                                PTS[:, :].rearrange("p (h n) -> p h n",
                                                    n=8)[:, :, 0:1],
                                ALU.max)
                        v.scalar_tensor_tensor(
                            KSC[:, 0:64].rearrange("p (h n) -> p h n", n=8),
                            tpv[:, :, 0, :], -2.0,
                            TS64[:, :].rearrange("p (h n) -> p h n", n=8),
                            ALU.mult, ALU.add)
                        v.tensor_add(KSC[:, 64:128], KSC[:, 0:64],
                                     POSSQ[:, :])
                        v.tensor_scalar_max(KSC[:, 64:128], KSC[:, 64:128],
                                            0.0)
                        v.tensor_tensor(
                            KSC[:, 128:192].rearrange("p (h n) -> p h n", n=8),
                            tpv[:, :, 1, :],
                            PDRr[:, :].rearrange("p (h n) -> p h n", n=8),
                            ALU.subtract)
                        v.tensor_mul(KSC[:, 128:192], KSC[:, 128:192],
                                     KSC[:, 128:192])
                        v.tensor_sub(KSC[:, 128:192], KSC[:, 64:128],
                                     KSC[:, 128:192])
                        v.tensor_scalar_max(KSC[:, 128:192], KSC[:, 128:192],
                                            0.0)
                        v.tensor_mul(KSC[:, 64:128], KSC[:, 64:128],
                                     NINVR[:, :])
                        v.tensor_mul(KSC[:, 128:192], KSC[:, 128:192],
                                     NINVR[:, :]).then_inc(KA1.h, 1)
                        v.wait_ge(KA2.h, ka20 + gk + 1)
                        v.tensor_scalar(KSC[:, 0:64], KSC[:, 64:128],
                                        OMC[:, 0:1], None, ALU.mult)
                        v.scalar_tensor_tensor(
                            KA[:, gk * 64:(gk + 1) * 64],
                            KSC[:, 128:192], DS4[:, 0:1], KSC[:, 0:64],
                            ALU.mult, ALU.add).then_inc(KAE.h, 1)
            EVK.n = evk0 + NSC * NHL
            KSQm.n = ksq0 + NSC * NHL
            EVV.n = evv0 + NSC * 2 * NVT
            KA1.n = ka10 + NSC * NVT
            KAE.n = kae0 + NSC * NVT

            @blk.scalar
            def _(sc_):
                sc_ = W(sc_, SM[21])
                for gk in range(NSC * NVT):
                    sc_.wait_ge(KA1.h, ka10 + gk + 1)
                    sc_.activation(KSC[:, 64:128], KSC[:, 64:128], AF.Exp)
                    sc_.activation(KSC[:, 128:192], KSC[:, 128:192],
                                   AF.Exp).then_inc(KA2.h, 1)
            KA2.n = ka20 + NSC * NVT

        if stop_after < 3:
            attn1.close()
            return nc, DH
        attn2 = ExitStack()
        QT = sb(attn2, "QT", [128, NHL * PR], BF16)
        ATN = sb(attn2, "ATN", [8, NHL * PR], BF16)

        PQ, QM, QE = SM[22], SM[6], SM[7]
        QA1, QA2, QAE = SM[14], SM[15], SM[16]
        TRm, TEm = SM[12], SM[13]
        NST_Q = R // 128
        with (nc.sbuf_tensor("XSB_SIB", [128, NCC * R], BF16) as XSB_SIB,
              nc.sbuf_tensor("WQS", [128, NCC * 1024], BF16) as WQS,
              nc.sbuf_tensor("QSQ", [128, NHL * R], BF16) as QSQ,
              nc.sbuf_tensor("TS64q", [128, 64], F32) as TS64q,
              nc.sbuf_tensor("QSC", [128, 4 * 64], F32) as QSC,
              nc.sbuf_tensor("QAO", [128, 64], BF16) as QAO,
              nc.psum_tensor("PQT", [128, 2 * R], F32) as PQT,
              nc.psum_tensor("PKAq", [128, 128], F32) as PKAq,
              nc.psum_tensor("PTSq", [128, 64], F32) as PTSq,
              nc.psum_tensor("PAT", [8, 128], BF16) as PAT,
              nc.Block() as blk):
            pq0, qm0, qe0 = PQ.n, QM.n, QE.n
            qa10, qa20, qae0 = QA1.n, QA2.n, QAE.n
            trm0, tem0 = TRm.n, TEm.n
            xl20 = XL.n

            @blk.gpsimd
            def _(g):
                with g.register("bt2") as bt, g.register("rt2") as rt:
                    g.reg_load(bt, IDXT[0:1, 2:3])
                    g.reg_load(rt, IDXT[0:1, 3:4])
                    ownb = g.snap(bt, donate=True, min_val=0, max_val=S - R)
                    sibb = g.snap(rt, donate=True, min_val=0, max_val=S - R)
                    g.dma_start(out=dr_pq[0:R, :],
                                in_=sh_xn[bass.ds(ownb, R), :]
                                ).then_inc(PQ.h, 16)
                    g.dma_start(out=dr_pq[R:2 * R, :],
                                in_=sh_xn[bass.ds(sibb, R), :]
                                ).then_inc(PQ.h, 16)
                for cj in range(NCC):
                    g.dma_start(out=WQS[:, cj * 1024:(cj + 1) * 1024],
                                in_=d_wq[cj * 128:(cj + 1) * 128, :]
                                ).then_inc(GL[0].h, 16)
            PQ.n = pq0 + 32
            GL[0].n += 16 * NCC

            @blk.sync
            def _(sy):
                sy.wait_ge(PQ.h, pq0 + 32)
                for cj in range(NCC):
                    sy.dma_start(out=XNT_OWN[:, cj * R:(cj + 1) * R],
                                 in_=dr_pq[0:R, cj * 128:(cj + 1) * 128],
                                 transpose=True).then_inc(XL.h, 16)
                for cj in range(NCC):
                    sy.dma_start(out=XSB_SIB[:, cj * R:(cj + 1) * R],
                                 in_=dr_pq[R:2 * R, cj * 128:(cj + 1) * 128],
                                 transpose=True).then_inc(XL.h, 16)
            XL.n = xl20 + 32 * NCC

            @blk.tensor
            def _(t):
                t = TW(t)
                t.wait_ge(XL.h, xl20 + 32 * NCC)
                t.wait_ge(GL[0].h, GL[0].n)
                for qc in range(2):
                    qsrc = XNT_OWN if qc == 0 else XSB_SIB
                    for h in range(NHL):
                        gi = qc * NHL + h
                        if gi >= 2:
                            t.wait_ge(QE.h, qe0 + gi - 1)
                        pb = (gi % 2) * R
                        for cj in range(NCC):
                            for hf in range(R // 512):
                                t.matmul(
                                    PQT[:, pb + hf * 512:pb + (hf + 1) * 512],
                                    WQS[:, cj * 1024 + h * 128:
                                        cj * 1024 + (h + 1) * 128],
                                    qsrc[:, cj * R + hf * 512:
                                         cj * R + (hf + 1) * 512],
                                    start=(cj == 0), stop=(cj == NCC - 1))
                        t.nop().then_inc(QM.h, 1)
                    t.wait_ge(QE.h, qe0 + (qc + 1) * NHL)
                    for st in range(NST_Q):
                        gk = qc * NST_Q + st
                        if gk >= 1:
                            t.wait_ge(QAE.h, qae0 + gk)
                        for h in range(NHL):
                            t.matmul(PKAq[:, h * 16:h * 16 + 8],
                                     QT[:, h * PR + qc * R + st * 128:
                                        h * PR + qc * R + (st + 1) * 128],
                                     POS_T[:, h * 8:(h + 1) * 8],
                                     start=True, stop=True)
                            t.matmul(PKAq[:, h * 16 + 8:(h + 1) * 16],
                                     QT[:, h * PR + qc * R + st * 128:
                                        h * PR + qc * R + (st + 1) * 128],
                                     DIR_T[:, h * 8:(h + 1) * 8],
                                     start=True, stop=True)
                            t.matmul(PTSq[:, h * 8:(h + 1) * 8],
                                     QSQ[:, h * R + st * 128:
                                         h * R + (st + 1) * 128],
                                     ONESB[:, 0:8],
                                     start=True, stop=True)
                        t.nop().then_inc(KAM.h, 1)
                        t.wait_ge(QAE.h, qae0 + gk + 1)
                        for h in range(NHL):
                            if gk * NHL + h >= 1:
                                t.wait_ge(TEm.h, tem0 + gk * NHL + h)
                            t.transpose(PAT[0:8, 0:128],
                                        QAO[:, h * 8:(h + 1) * 8],
                                        IDENT[:, :])
                            t.nop().then_inc(TRm.h, 1)
            QM.n = qm0 + 2 * NHL
            kam_b = KAM.n
            KAM.n += 2 * NST_Q
            TRm.n = trm0 + 2 * NST_Q * NHL

            @blk.vector
            def _(v):
                v = W(v, SM[20])
                for qc in range(2):
                    for h in range(NHL):
                        gi = qc * NHL + h
                        v.wait_ge(QM.h, qm0 + gi + 1)
                        if qc >= 1 and h == 0:
                            v.wait_ge(KAM.h, kam_b + qc * NST_Q)
                        pb = (gi % 2) * R
                        v.tensor_copy(QT[:, h * PR + qc * R:
                                         h * PR + qc * R + R],
                                      PQT[:, pb:pb + R])
                        v.tensor_mul(QSQ[:, h * R:(h + 1) * R],
                                     QT[:, h * PR + qc * R:
                                        h * PR + qc * R + R],
                                     QT[:, h * PR + qc * R:
                                        h * PR + qc * R + R]
                                     ).then_inc(QE.h, 1)
                    for st in range(NST_Q):
                        gk = qc * NST_Q + st
                        v.wait_ge(KAM.h, kam_b + gk + 1)
                        tpv = PKAq[:, :].rearrange("p (h w n) -> p h w n",
                                                   w=2, n=8)
                        v.tensor_copy(TS64q[:, :], PTSq[:, :])
                        if gk == 0:
                            v.tensor_copy(
                                MXQ[:, :].rearrange("p (h o) -> p h o", o=1),
                                PTSq[:, :].rearrange("p (h n) -> p h n",
                                                     n=8)[:, :, 0:1])
                        else:
                            v.tensor_tensor(
                                MXQ[:, :].rearrange("p (h o) -> p h o", o=1),
                                MXQ[:, :].rearrange("p (h o) -> p h o", o=1),
                                PTSq[:, :].rearrange("p (h n) -> p h n",
                                                     n=8)[:, :, 0:1],
                                ALU.max)
                        v.scalar_tensor_tensor(
                            QSC[:, 0:64].rearrange("p (h n) -> p h n", n=8),
                            tpv[:, :, 0, :], -2.0,
                            TS64q[:, :].rearrange("p (h n) -> p h n", n=8),
                            ALU.mult, ALU.add)
                        v.tensor_add(QSC[:, 64:128], QSC[:, 0:64],
                                     POSSQ[:, :])
                        v.tensor_scalar_max(QSC[:, 64:128], QSC[:, 64:128],
                                            0.0)
                        v.tensor_tensor(
                            QSC[:, 128:192].rearrange("p (h n) -> p h n", n=8),
                            tpv[:, :, 1, :],
                            PDRr[:, :].rearrange("p (h n) -> p h n", n=8),
                            ALU.subtract)
                        v.tensor_mul(QSC[:, 128:192], QSC[:, 128:192],
                                     QSC[:, 128:192])
                        v.tensor_sub(QSC[:, 128:192], QSC[:, 64:128],
                                     QSC[:, 128:192])
                        v.tensor_scalar_max(QSC[:, 128:192], QSC[:, 128:192],
                                            0.0)
                        v.tensor_mul(QSC[:, 64:128], QSC[:, 64:128],
                                     NINVR[:, :])
                        v.tensor_mul(QSC[:, 128:192], QSC[:, 128:192],
                                     NINVR[:, :]).then_inc(QA1.h, 1)
                        v.wait_ge(QA2.h, qa20 + gk + 1)
                        v.tensor_scalar(QSC[:, 0:64], QSC[:, 64:128],
                                        OMC[:, 0:1], None, ALU.mult)
                        v.scalar_tensor_tensor(
                            QSC[:, 192:256], QSC[:, 128:192], DS4[:, 0:1],
                            QSC[:, 0:64], ALU.mult, ALU.add)
                        v.tensor_tensor(QAO[:, :], QSC[:, 192:256],
                                        AMPR[:, :], ALU.mult
                                        ).then_inc(QAE.h, 1)
                        for h in range(NHL):
                            v.wait_ge(TRm.h, trm0 + gk * NHL + h + 1)
                            v.tensor_copy(
                                ATN[0:8, h * PR + qc * R + st * 128:
                                    h * PR + qc * R + (st + 1) * 128],
                                PAT[0:8, 0:128]).then_inc(TEm.h, 1)
            QE.n = qe0 + 2 * NHL
            QA1.n = qa10 + 2 * NST_Q
            QAE.n = qae0 + 2 * NST_Q
            TEm.n = tem0 + 2 * NST_Q * NHL

            @blk.scalar
            def _(sc_):
                sc_ = W(sc_, SM[21])
                for gk in range(2 * NST_Q):
                    sc_.wait_ge(QA1.h, qa10 + gk + 1)
                    sc_.activation(QSC[:, 64:128], QSC[:, 64:128], AF.Exp)
                    sc_.activation(QSC[:, 128:192], QSC[:, 128:192],
                                   AF.Exp).then_inc(QA2.h, 1)
            QA2.n = qa20 + 2 * NST_Q

        NB = SM[0]
        with (nc.sbuf_tensor("MC8", [8, 4], F32) as MC8,
              nc.sbuf_tensor("NR8", [1, 8], F32) as NR8,
              nc.psum_tensor("PMX", [8, 128], F32) as PMX,
              nc.psum_tensor("PR8", [1, 8], F32) as PR8,
              nc.psum_tensor("PNB", [128, 8], F32) as PNB,
              nc.Block() as blk):
            nb0 = NB.n

            @blk.tensor
            def _(t):
                t = TW(t)
                t.transpose(PMX[0:8, 0:128], MXK[:, :], IDENTF[:, :])
                t.nop().then_inc(NB.h, 1)
                t.wait_ge(NB.h, nb0 + 2)
                t.transpose(PMX[0:8, 0:128], MXQ[:, :], IDENTF[:, :])
                t.nop().then_inc(NB.h, 1)
                t.wait_ge(NB.h, nb0 + 8)
                t.transpose(PR8[0:1, 0:8], MC8[0:8, 3:4], IDENTF[0:8, 0:8])
                t.nop().then_inc(NB.h, 1)
                t.wait_ge(NB.h, nb0 + 10)
                t.matmul(PNB[:, 0:8], ONESF[0:1, :], NR8[0:1, :],
                         start=True, stop=True)
                t.nop().then_inc(NB.h, 1)

            @blk.vector
            def _(v):
                v = W(v, SM[20])
                v.wait_ge(NB.h, nb0 + 1)
                v.reduce_max(MC8[0:8, 0:1], PMX[0:8, 0:128], axis=AX.X)
                v.nop().then_inc(NB.h, 1)
                v.wait_ge(NB.h, nb0 + 3)
                v.reduce_max(MC8[0:8, 1:2], PMX[0:8, 0:128], axis=AX.X)
                v.nop().then_inc(NB.h, 1)
                v.wait_ge(NB.h, nb0 + 5)
                v.tensor_add(MC8[0:8, 3:4], MC8[0:8, 3:4], MC8[0:8, 2:3])
                v.nop().then_inc(NB.h, 1)
                v.wait_ge(NB.h, nb0 + 7)
                v.tensor_scalar_mul(MC8[0:8, 3:4], MC8[0:8, 3:4], -1.0)
                v.nop().then_inc(NB.h, 1)
                v.wait_ge(NB.h, nb0 + 9)
                v.tensor_copy(NR8[0:1, :], PR8[0:1, 0:8])
                v.nop().then_inc(NB.h, 1)
                v.wait_ge(NB.h, nb0 + 11)
                v.tensor_copy(NCBT[:, :], PNB[:, 0:8])
                v.nop().then_inc(NB.h, 1)

            @blk.scalar
            def _(sc_):
                sc_ = W(sc_, SM[21])
                sc_.wait_ge(NB.h, nb0 + 4)
                sc_.activation(MC8[0:8, 2:3], MC8[0:8, 0:1], AF.Ln)
                sc_.activation(MC8[0:8, 3:4], MC8[0:8, 1:2], AF.Ln)
                sc_.nop().then_inc(NB.h, 1)
                sc_.wait_ge(NB.h, nb0 + 6)
                sc_.activation(MC8[0:8, 3:4], MC8[0:8, 3:4], AF.Exp,
                               scale=0.5, bias=FCN[0:8, 1:2])
                sc_.nop().then_inc(NB.h, 1)
            NB.n = nb0 + 12

        if "qt" in DH:
            with nc.Block() as blk:
                @blk.sync
                def _(sy):
                    for dst, src in [(DH["qt"][:, :], QT[:, :]),
                                     (DH["at"][:, :], ATN[0:8, :]),
                                     (DH["ncb"][:, :], NCBT[:, :])]:
                        sy.dma_start(out=dst, in_=src).then_inc(NB.h, 16)
                        NB.n += 16
                    sy.wait_ge(NB.h, NB.n)

        if stop_after < 4:
            attn2.close()
            attn1.close()
            return nc, DH
        KV, SCM, PT_, PV_, DG, CB = SM[0], SM[2], SM[3], SM[4], SM[5], SM[6]
        MKM, MKE, HDs, CST = SM[7], SM[9], SM[10], SM[23]
        with (nc.sbuf_tensor("KTB", [128, 2 * S], BF16) as KTB,
              nc.sbuf_tensor("VB", [128, 2 * NKT * VTW], BF16) as VB,
              nc.sbuf_tensor("PTR", [128, 4 * 1024], BF16) as PTRb,
              nc.sbuf_tensor("MKVS", [8, VTW], BF16) as MKVS,
              nc.sbuf_tensor("CCOL", [128, 8], F32) as CCOL,
              nc.sbuf_tensor("CT1", [128, 128], F32) as CT1,
              nc.sbuf_tensor("CTO", [128, 4 * 128], BF16) as CTO,
              nc.psum_tensor("PSC", [128, 2048], F32) as PSC,
              nc.psum_tensor("PPVA", [128, 132], F32) as PPVA,
              nc.psum_tensor("PPVB", [128, 132], F32) as PPVB,
              nc.psum_tensor("PMK", [8, 132], F32) as PMK,
              nc.psum_tensor("PDG", [128, 132], F32) as PDG,
              nc.Block() as blk):
            kv0, scm0, pt0, pv0 = KV.n, SCM.n, PT_.n, PV_.n
            kvsl0 = [SL[8].n, SL[9].n]
            cst_sl0 = [GL[6].n, GL[7].n, GL[8].n, GL[9].n]
            dg0, cb0, mkm0, mke0 = DG.n, CB.n, MKM.n, MKE.n
            hd0, cst0 = HDs.n, CST.n

            NQB2 = PR // 256
            NJJ = NKT // 4

            def G2(h, qb, jj):
                return (h * NQB2 + qb) * NJJ + jj

            def Q2(h, qb):
                return h * NQB2 + qb

            @blk.vector
            def _(v):
                v = W(v, SM[20])
                for b in range(2):
                    v.memset(VB[:, :].rearrange(
                        "p (t w) -> p t w", w=VTW)[:, b * NKT:(b + 1) * NKT,
                                                   128:129], 1.0)
                v.nop().then_inc(SM[0].h, 1)
                for h in range(NHL):
                    v.wait_ge(MKM.h, mkm0 + h + 1)
                    if h >= 1:
                        v.wait_ge(DG.h, dg0 + h * NQB2 * 2)
                    v.tensor_copy(MKVS[0:8, 0:129], PMK[0:8, 0:129]
                                  ).then_inc(MKE.h, 1)
                    for qb in range(NQB2):
                        v.wait_ge(PV_.h, pv0 + (Q2(h, qb) + 1) * NJJ)
                        for sub in range(2):
                            i = Q2(h, qb) * 2 + sub
                            v.wait_ge(DG.h, dg0 + i + 1)
                            if i >= 4:
                                v.wait_ge(GL[6 + i % 4].h,
                                          cst_sl0[i % 4] + 16 * (i // 4))
                            pp = PPVA if sub == 0 else PPVB
                            v.reciprocal(CCOL[:, 0:1], pp[:, 128:129])
                            v.tensor_mul(CCOL[:, 1:2], CCOL[:, 0:1],
                                         OMC[:, 1:2])
                            v.tensor_scalar_add(CCOL[:, 2:3],
                                                PDG[:, 128:129], 1e-8)
                            v.reciprocal(CCOL[:, 2:3], CCOL[:, 2:3])
                            v.tensor_mul(CCOL[:, 3:4], CCOL[:, 2:3],
                                         DS4[:, 1:2])
                            v.tensor_scalar(CT1[:, :], PDG[:, 0:128],
                                            CCOL[:, 3:4], None, ALU.mult)
                            v.scalar_tensor_tensor(
                                CTO[:, (i % 4) * 128:(i % 4 + 1) * 128],
                                pp[:, 0:128], CCOL[:, 1:2], CT1[:, :],
                                ALU.mult, ALU.add).then_inc(CB.h, 1)
            MKE.n = mke0 + NHL
            CB.n = cb0 + NHL * (PR // 128)

            @blk.gpsimd
            def _(g):
                for h in range(NHL):
                    for qb in range(NQB2):
                        for sub in range(2):
                            i = Q2(h, qb) * 2 + sub
                            g.wait_ge(CB.h, cb0 + i + 1)
                            qt = qb * 2 + sub
                            g.dma_start(out=dr_ctx[qt, :,
                                                   h * 128:(h + 1) * 128],
                                        in_=CTO[:, (i % 4) * 128:
                                                (i % 4 + 1) * 128]
                                        ).then_inc(GL[6 + i % 4].h, 16)
                NCTX = NHL * (PR // 128)
                for q in range(4):
                    g.wait_ge(GL[6 + q].h, GL[6 + q].n +
                              16 * ((NCTX - q + 3) // 4))
            for i in range(NHL * (PR // 128)):
                GL[6 + i % 4].n += 16

            @blk.sync
            def _(sy):
                sy.wait_ge(SM[0].h, SM[0].n + 1)
                for q in range(2, 6):
                    sy.wait_ge(GL[q].h, GL[q].n)
                for h in range(NHL):
                    b = h % 2
                    if h >= 2:
                        sy.wait_ge(SCM.h, scm0 +
                                   (h - 1) * (PR // 256) * (NKT // 4))
                        sy.wait_ge(PV_.h, pv0 +
                                   (h - 1) * (PR // 256) * (NKT // 4))
                        sy.wait_ge(MKM.h, mkm0 + h - 1)
                    sy.dma_start(out=KTB[:, b * S:(b + 1) * S],
                                 in_=dr_kt[h, :, :]).then_inc(SL[8 + b].h, 16)
                    grp, sub4 = divmod(h, 4)
                    sy.dma_start(
                        out=VB[:, :].rearrange(
                            "p (t w) -> p t w", w=VTW)[:, b * NKT:(b + 1) * NKT,
                                                       0:128],
                        in_=dr_v[grp, :, :].rearrange(
                            "(t p) f -> p t f", p=128)[:, :, sub4 * 128:
                                                       (sub4 + 1) * 128]
                        ).then_inc(SL[8 + b].h, 16)
                if "ctx" in DH:
                    NCTX2 = NHL * NQB * 4
                    for q in range(4):
                        sy.wait_ge(GL[6 + q].h, cst_sl0[q] +
                                   16 * ((NCTX2 - q + 3) // 4))
                    sy.dma_start(out=DH["ctx"].ap(), in_=dr_ctx.ap()
                                 ).then_inc(SL[0].h, 16)
                    SL[0].n += 16
                    sy.wait_ge(SL[0].h, SL[0].n)
            for h in range(NHL):
                SL[8 + h % 2].n += 32
            SM[0].n += 1

            @blk.tensor
            def _(t):
                t = TW(t)
                for h in range(NHL):
                    b = h % 2
                    t.wait_ge(SL[8 + b].h, kvsl0[b] + 32 * (h // 2 + 1))
                    if h >= 1:
                        t.wait_ge(MKE.h, mke0 + h)
                    for kt in range(NKT):
                        t.matmul(PMK[0:8, 0:129],
                                 KA[:, kt * 64 + h * 8:kt * 64 + (h + 1) * 8],
                                 VB[:, (b * NKT + kt) * VTW:
                                    (b * NKT + kt) * VTW + 129],
                                 start=(kt == 0), stop=(kt == NKT - 1))
                    t.nop().then_inc(MKM.h, 1)
                    for qb in range(NQB2):
                        for jj in range(NJJ):
                            g = G2(h, qb, jj)
                            if g >= 2:
                                t.wait_ge(PT_.h, pt0 + g - 1)
                            sl = (jj % 2) * 1024
                            for u in range(4):
                                kt = 4 * jj + u
                                t.matmul(PSC[:, sl + u * 256:
                                             sl + (u + 1) * 256],
                                         KTB[:, b * S + kt * 128:
                                             b * S + (kt + 1) * 128],
                                         QT[:, h * PR + qb * 256:
                                            h * PR + (qb + 1) * 256],
                                         start=True, stop=True)
                            t.nop().then_inc(SCM.h, 1)
                            t.wait_ge(PT_.h, pt0 + g + 1)
                            if jj == 0:
                                t.wait_ge(CB.h, cb0 + 2 * Q2(h, qb))
                            sl4 = (jj % 4) * 1024
                            for u in range(4):
                                kt = 4 * jj + u
                                for sub in range(2):
                                    pp = PPVA if sub == 0 else PPVB
                                    t.matmul(
                                        pp[:, 0:129],
                                        PTRb[:, sl4 + u * 256 + sub * 128:
                                             sl4 + u * 256 + (sub + 1) * 128],
                                        VB[:, (b * NKT + kt) * VTW:
                                           (b * NKT + kt) * VTW + 129],
                                        start=(kt == 0),
                                        stop=(kt == NKT - 1))
                            t.nop().then_inc(PV_.h, 1)
                        t.wait_ge(MKE.h, mke0 + h + 1)
                        for sub in range(2):
                            i = Q2(h, qb) * 2 + sub
                            if i >= 1:
                                t.wait_ge(CB.h, cb0 + i)
                            t.matmul(PDG[:, 0:129],
                                     ATN[0:8, h * PR + qb * 256 + sub * 128:
                                         h * PR + qb * 256 + (sub + 1) * 128],
                                     MKVS[0:8, 0:129],
                                     start=True, stop=True)
                            t.nop().then_inc(DG.h, 1)
            MKM.n = mkm0 + NHL
            SCM.n = scm0 + NHL * (PR // 256) * (NKT // 4)
            PV_.n = pv0 + NHL * (PR // 256) * (NKT // 4)
            DG.n = dg0 + NHL * (PR // 128)

            @blk.scalar
            def _(sc_):
                sc_ = W(sc_, SM[21])
                for h in range(NHL):
                    for qb in range(NQB2):
                        for jj in range(NJJ):
                            g = G2(h, qb, jj)
                            sc_.wait_ge(SCM.h, scm0 + g + 1)
                            if g >= 4:
                                sc_.wait_ge(PV_.h, pv0 + g - 3)
                            sl = (jj % 2) * 1024
                            sl4 = (jj % 4) * 1024
                            sc_.activation(PTRb[:, sl4:sl4 + 1024],
                                           PSC[:, sl:sl + 1024], AF.Exp,
                                           bias=NCBT[:, h:h + 1],
                                           scale=ISQD).then_inc(PT_.h, 1)
            PT_.n = pt0 + NHL * (PR // 256) * (NKT // 4)

        attn2.close()
        attn1.close()

        if stop_after < 5:
            return nc, DH
        late = ExitStack()
        SSB = sb(late, "SSB", [128, NOT_ * DM], BF16)

        XCH, WO, CTF, CTT, SEV, SMM = SM[22], SM[25], SM[24], SM[4], SM[5], SM[6]
        with (nc.sbuf_tensor("CTFb", [128, 2 * DM], BF16) as CTFb,
              nc.sbuf_tensor("WOS", [128, NCC * DM], BF16) as WOS,
              nc.sbuf_tensor("CTTb", [128, 2 * NCC * 128], BF16) as CTTb,
              nc.psum_tensor("PTT", [128, 128], BF16) as PTT,
              nc.psum_tensor("PS5", [128, 2 * 512], F32) as PS5,
              nc.Block() as blk):
            xch0, wo0, ctf0, ctt0 = XCH.n, WO.n, CTF.n, CTT.n
            ctf_sl0 = [GL[10].n, GL[11].n]
            sev0, smm0 = SEV.n, SMM.n

            SSE = SM[8]
            sse0 = SSE.n
            sh_flat = sh_ctx.ap().rearrange("w r c -> (w r) c")

            XONLY = cfg.get("xch_only", 0)

            @blk.gpsimd
            def _(g):
                for cj in range(NCC):
                    g.dma_start(out=WOS[:, cj * DM:(cj + 1) * DM],
                                in_=d_wo[cj * 128:(cj + 1) * 128, :]
                                ).then_inc(WO.h, 16)
                for q in range(6, 10):
                    g.wait_ge(GL[q].h, GL[q].n)
                dc_flat = dr_ctx.ap().rearrange("t p c -> (t p) c")
                with (g.register("pr4") as pr, g.register("sib4") as sib,
                      g.register("po4") as po, g.register("so4") as so):
                    g.reg_load(pr, IDXT[0:1, 0:1])
                    g.reg_load(sib, IDXT[0:1, 1:2])
                    g.reg_alu(po, pr, R, ALU.mult)
                    g.reg_alu(so, sib, R, ALU.mult)
                    pov = g.snap(po, min_val=0, max_val=R)
                    sob = g.snap(so, min_val=0, max_val=R)
                    if XONLY != 4:
                        g.dma_start(out=sh_flat[bass.ds(pov, R), :],
                                    in_=dc_flat[R:2 * R, :]
                                    ).then_inc(XCH.h, 16)
                        XCH.n = xch0 + 16
                    if XONLY in (0, 1, 2):
                        barrier(g, 2, [(XCH.h, XCH.n)], XCH)
                    elif XONLY == 3:
                        g.wait_ge(XCH.h, XCH.n)
                    if XONLY in (0, 1):
                        g.dma_start(out=dr_sib[:, :],
                                    in_=sh_flat[bass.ds(sob, R), :]
                                    ).then_inc(XCH.h, 16)
                        XCH.n += 16
                        g.wait_ge(XCH.h, XCH.n)
                for tt in range(0 if XONLY in (2, 3, 4) else
                                (2 if XONLY else NOT_)):
                    if tt >= 2:
                        g.wait_ge(CTT.h, ctt0 + (tt - 1) * NCC)
                    bb = (tt % 2) * DM
                    g.dma_start(out=CTFb[:, bb:bb + HW],
                                in_=dr_ctx[tt, :, :]
                                ).then_inc(GL[10 + tt % 2].h, 16)
                    g.dma_start(out=CTFb[:, bb + HW:bb + DM],
                                in_=dr_sib[tt * 128:(tt + 1) * 128, :]
                                ).then_inc(GL[10 + tt % 2].h, 16)
            for tt in range(0 if XONLY in (2, 3, 4) else
                            (2 if XONLY else NOT_)):
                GL[10 + tt % 2].n += 32
            WO.n = wo0 + 16 * NCC

            if XONLY:
                @blk.sync
                def _(sy):
                    sy.wait_ge(WO.h, wo0 + 16 * NCC)
                    sy.wait_ge(GL[10].h, GL[10].n)
                    sy.wait_ge(GL[11].h, GL[11].n)
                    if XONLY != 4:
                        sy.wait_ge(XCH.h, XCH.n)
                    sy.dma_start(out=DH["xch"][:, :], in_=CTFb[:, :]
                                 ).then_inc(SL[0].h, 16)
                    SL[0].n += 16
                    sy.wait_ge(SL[0].h, SL[0].n)

            def sec(engdec):
                return (lambda f: None) if XONLY else engdec

            @sec(blk.tensor)
            def _(t):
                t = TW(t)
                t.wait_ge(WO.h, wo0 + 16 * NCC)
                for tt in range(NOT_):
                    t.wait_ge(GL[10 + tt % 2].h,
                              ctf_sl0[tt % 2] + 32 * (tt // 2 + 1))
                    bb = (tt % 2) * DM
                    cb_ = (tt % 2) * NCC * 128
                    for j in range(NCC):
                        i = tt * NCC + j
                        if i >= 1:
                            t.wait_ge(SEV.h, sev0 + i)
                        t.transpose(PTT[:, :],
                                    CTFb[:, bb + j * 128:bb + (j + 1) * 128],
                                    IDENT[:, :])
                        t.nop().then_inc(CTT.h, 1)
                    t.wait_ge(SEV.h, sev0 + (tt + 1) * NCC)
                    for cc in range(NC5):
                        u = tt * NC5 + cc
                        if u >= 2:
                            t.wait_ge(SSE.h, sse0 + u - 1)
                        for j in range(NCC):
                            t.matmul(PS5[:, (u % 2) * 512:(u % 2 + 1) * 512],
                                     CTTb[:, cb_ + j * 128:
                                          cb_ + (j + 1) * 128],
                                     WOS[:, j * DM + cc * 512:
                                         j * DM + (cc + 1) * 512],
                                     start=(j == 0), stop=(j == NCC - 1))
                        t.nop().then_inc(SMM.h, 1)
            if not XONLY:
                CTT.n = ctt0 + NOT_ * NCC
                SMM.n = smm0 + NOT_ * NC5

            @sec(blk.vector)
            def _(v):
                v = W(v, SM[20])
                for tt in range(NOT_):
                    for j in range(NCC):
                        i = tt * NCC + j
                        v.wait_ge(CTT.h, ctt0 + i + 1)
                        if j == 0 and tt >= 2:
                            v.wait_ge(SMM.h, smm0 + (tt - 1) * NC5)
                        v.tensor_copy(CTTb[:, (tt % 2) * NCC * 128 + j * 128:
                                           (tt % 2) * NCC * 128 + (j + 1) * 128],
                                      PTT[:, :]).then_inc(SEV.h, 1)
                    for cc in range(NC5):
                        u = tt * NC5 + cc
                        v.wait_ge(SMM.h, smm0 + u + 1)
                        v.tensor_copy(SSB[:, tt * DM + cc * 512:
                                          tt * DM + (cc + 1) * 512],
                                      PS5[:, (u % 2) * 512:(u % 2 + 1) * 512]
                                      ).then_inc(SSE.h, 1)
            if not XONLY:
                SEV.n = sev0 + NOT_ * NCC
                SSE.n = sse0 + NOT_ * NC5

            if "s" in DH and not XONLY:
                @blk.sync
                def _(sy):
                    sy.wait_ge(SSE.h, sse0 + NOT_ * NC5)
                    sy.dma_start(out=DH["s"][:, :], in_=SSB[:, :]
                                 ).then_inc(SL[0].h, 16)
                    SL[0].n += 16
                    sy.wait_ge(SL[0].h, SL[0].n)



        if stop_after < 6:
            late.close()
            return nc, DH
        fstk = ExitStack()
        FUSE = sb(fstk, "FUSE", [128, NIT * 512], BF16)
        OU = SM[19]
        DNC = SM[18]
        NHF = R // 512

        for hf in range(NHF):
            AW, ACV, AGM, AUM, ASL, AFU = (SM[0], SM[2], SM[3], SM[4],
                                           SM[5], SM[6])
            with (nc.sbuf_tensor("WGI%d" % hf, [128, 2 * NCC * 512],
                                  I8) as WGI,
                  nc.sbuf_tensor("WUI%d" % hf, [128, 2 * NCC * 512],
                                  I8) as WUI,
                  nc.sbuf_tensor("WGB%d" % hf, [128, NCC * 512],
                                  BF16) as WGB,
                  nc.sbuf_tensor("WUB%d" % hf, [128, NCC * 512],
                                  BF16) as WUB,
                  nc.sbuf_tensor("GSS%d" % hf, [128, 512], BF16) as GSS,
                  nc.psum_tensor("PG%d" % hf, [128, 512], F32) as PG,
                  nc.psum_tensor("PU%d" % hf, [128, 512], F32) as PU,
                  nc.Block() as blk):
                aw0, acv0, agm0 = AW.n, ACV.n, AGM.n
                aw_sl0 = [SL[10].n, SL[11].n]
                aum0, asl0, afu0 = AUM.n, ASL.n, AFU.n
                wgr = d_wg.ap().rearrange("(cj p) i -> p cj i", p=128)
                wur = d_wu.ap().rearrange("(cj p) i -> p cj i", p=128)

                @blk.sync
                def _(sy):
                    for m in range(NIM):
                        if m >= 2:
                            sy.wait_ge(ACV.h, acv0 + 2 * (m - 1))
                        b = m % 2
                        sy.dma_start(out=WGI[:, b * NCC * 512:
                                             (b + 1) * NCC * 512],
                                     in_=wgr[:, :, m * 512:(m + 1) * 512]
                                     ).then_inc(SL[10 + b].h, 16)
                        sy.dma_start(out=WUI[:, b * NCC * 512:
                                             (b + 1) * NCC * 512],
                                     in_=wur[:, :, m * 512:(m + 1) * 512]
                                     ).then_inc(SL[10 + b].h, 16)
                for m in range(NIM):
                    SL[10 + m % 2].n += 32

                @blk.scalar
                def _(sc_):
                    sc_ = W(sc_, SM[21])
                    for m in range(NIM):
                        sc_.wait_ge(SL[10 + m % 2].h,
                                    aw_sl0[m % 2] + 32 * (m // 2 + 1))
                        if m >= 1:
                            sc_.wait_ge(AGM.h, agm0 + m * 4)
                        b = m % 2
                        sc_.activation(WGB[:, :],
                                       WGI[:, b * NCC * 512:
                                           (b + 1) * NCC * 512],
                                       AF.Copy).then_inc(ACV.h, 1)
                        for itl in range(4):
                            it = m * 4 + itl
                            gg = it
                            sc_.wait_ge(AGM.h, agm0 + gg + 1)
                            if gg >= 1:
                                sc_.wait_ge(AFU.h, afu0 + gg)
                            sc_.activation(GSS[:, :], PG[:, :], AF.Silu,
                                           scale=SGC[:, it:it + 1]
                                           ).then_inc(ASL.h, 1)
                ACV.n = acv0 + NIM
                ASL.n = asl0 + NIM * 4

                @blk.vector
                def _(v):
                    v = W(v, SM[20])
                    for m in range(NIM):
                        v.wait_ge(SL[10 + m % 2].h,
                                  aw_sl0[m % 2] + 32 * (m // 2 + 1))
                        if m >= 1:
                            v.wait_ge(AUM.h, aum0 + m * 4)
                        b = m % 2
                        v.tensor_copy(WUB[:, :],
                                      WUI[:, b * NCC * 512:
                                          (b + 1) * NCC * 512]
                                      ).then_inc(ACV.h, 1)
                        for itl in range(4):
                            it = m * 4 + itl
                            v.wait_ge(ASL.h, asl0 + it + 1)
                            v.wait_ge(AUM.h, aum0 + it + 1)
                            v.scalar_tensor_tensor(
                                FUSE[:, it * 512:(it + 1) * 512],
                                PU[:, :], SUC[:, it:it + 1], GSS[:, :],
                                ALU.mult, ALU.mult).then_inc(AFU.h, 1)
                ACV.n += NIM
                AFU.n = afu0 + NIM * 4

                @blk.tensor
                def _(t):
                    t = TW(t)
                    for m in range(NIM):
                        t.wait_ge(ACV.h, acv0 + 2 * (m + 1))
                        for itl in range(4):
                            it = m * 4 + itl
                            if it >= 1:
                                t.wait_ge(ASL.h, asl0 + it)
                            for cj in range(NCC):
                                t.matmul(PG[:, :],
                                         WGB[:, cj * 512 + itl * 128:
                                             cj * 512 + (itl + 1) * 128],
                                         XNT_OWN[:, cj * R + hf * 512:
                                                 cj * R + (hf + 1) * 512],
                                         start=(cj == 0), stop=(cj == NCC - 1))
                            t.nop().then_inc(AGM.h, 1)
                            if it >= 1:
                                t.wait_ge(AFU.h, afu0 + it)
                            for cj in range(NCC):
                                t.matmul(PU[:, :],
                                         WUB[:, cj * 512 + itl * 128:
                                             cj * 512 + (itl + 1) * 128],
                                         XNT_OWN[:, cj * R + hf * 512:
                                                 cj * R + (hf + 1) * 512],
                                         start=(cj == 0), stop=(cj == NCC - 1))
                            t.nop().then_inc(AUM.h, 1)
                AGM.n = agm0 + NIM * 4
                AUM.n = aum0 + NIM * 4

            BW, BCV, BDN, BSL, BTR, BTE, BXL = (SM[0], SM[2], SM[3], SM[4],
                                                SM[5], SM[6], SM[7])
            with (nc.sbuf_tensor("WDI%d" % hf, [128, 2 * NIT * 128],
                                  I8) as WDI,
                  nc.sbuf_tensor("WDB%d" % hf, [128, NIT * 128],
                                  BF16) as WDB,
                  nc.sbuf_tensor("GSD%d" % hf, [128, 512], BF16) as GSD,
                  nc.sbuf_tensor("GOB%d" % hf, [128, 4 * DM], BF16) as GOB,
                  nc.sbuf_tensor("T2B%d" % hf, [128, DM], F32) as T2B,
                  nc.sbuf_tensor("XOB%d" % hf, [128, 2 * DM], F32) as XOB,
                  nc.psum_tensor("PDN%d" % hf, [128, 2 * 512], F32) as PDN,
                  nc.psum_tensor("PTR2%d" % hf, [128, 128], BF16) as PTR2,
                  nc.Block() as blk):
                bw0, bcv0, bdn0 = BW.n, BCV.n, BDN.n
                bw_sl0 = [SL[12].n, SL[13].n]
                xl_sl0 = [SL[14].n, SL[15].n]
                ou_sl0 = [SL[16].n, SL[17].n]
                bsl0, btr0, bte0, bxl0 = BSL.n, BTR.n, BTE.n, BXL.n
                dnc0, ou0 = DNC.n, OU.n
                wdr = d_wd.ap().rearrange("(ic p) c -> p ic c", p=128)

                @blk.sync
                def _(sy):
                    for ct in range(NCT):
                        if ct >= 2:
                            sy.wait_ge(BCV.h, bcv0 + ct - 1)
                        b = ct % 2
                        sy.dma_start(out=WDI[:, b * NIT * 128:
                                             (b + 1) * NIT * 128],
                                     in_=wdr[:, :, ct * 128:(ct + 1) * 128]
                                     ).then_inc(SL[12 + b].h, 16)
                    for qi in range(2):
                        qt = hf * 4 + qi
                        sy.dma_start(out=XOB[:, (qi % 2) * DM:
                                             (qi % 2 + 1) * DM],
                                     in_=d_xown[qt * 128:(qt + 1) * 128, :]
                                     ).then_inc(SL[14 + qi % 2].h, 16)
                    for qi in range(4):
                        qt = hf * 4 + qi
                        sy.wait_ge(DNC.h, dnc0 + qi + 1)
                        sy.dma_start(out=d_out[qt * 128:(qt + 1) * 128, :],
                                     in_=XOB[:, (qi % 2) * DM:
                                             (qi % 2 + 1) * DM]
                                     ).then_inc(SL[16 + qi % 2].h, 16)
                        if qi + 2 < 4:
                            sy.wait_ge(SL[16 + qi % 2].h,
                                       ou_sl0[qi % 2] + 16 * (qi // 2 + 1))
                            sy.dma_start(out=XOB[:, (qi % 2) * DM:
                                                 (qi % 2 + 1) * DM],
                                         in_=d_xown[(qt + 2) * 128:
                                                    (qt + 3) * 128, :]
                                         ).then_inc(SL[14 + qi % 2].h, 16)
                    for qi in range(4):
                        SL[16 + qi % 2].n += 16
                    sy.wait_ge(SL[16].h, SL[16].n)
                    sy.wait_ge(SL[17].h, SL[17].n)
                for ct in range(NCT):
                    SL[12 + ct % 2].n += 16
                for qi in range(4):
                    SL[14 + qi % 2].n += 16

                @blk.scalar
                def _(sc_):
                    sc_ = W(sc_, SM[21])
                    for ct in range(NCT):
                        sc_.wait_ge(SL[12 + ct % 2].h,
                                    bw_sl0[ct % 2] + 16 * (ct // 2 + 1))
                        if ct >= 1:
                            sc_.wait_ge(BDN.h, bdn0 + ct)
                        b = ct % 2
                        sc_.activation(WDB[:, :],
                                       WDI[:, b * NIT * 128:
                                           (b + 1) * NIT * 128],
                                       AF.Copy).then_inc(BCV.h, 1)
                        sc_.wait_ge(BDN.h, bdn0 + ct + 1)
                        if ct >= 1:
                            sc_.wait_ge(BTR.h, btr0 + 4 * ct)
                        sc_.activation(GSD[:, :],
                                       PDN[:, (ct % 2) * 512:
                                           (ct % 2 + 1) * 512],
                                       AF.Silu, scale=SDCc[:, ct:ct + 1]
                                       ).then_inc(BSL.h, 1)
                BCV.n = bcv0 + NCT
                BSL.n = bsl0 + NCT

                @blk.tensor
                def _(t):
                    t = TW(t)
                    for ct in range(NCT):
                        t.wait_ge(BCV.h, bcv0 + ct + 1)
                        for ic in range(NIT):
                            t.matmul(PDN[:, (ct % 2) * 512:
                                         (ct % 2 + 1) * 512],
                                     WDB[:, ic * 128:(ic + 1) * 128],
                                     FUSE[:, ic * 512:(ic + 1) * 512],
                                     start=(ic == 0), stop=(ic == NIT - 1))
                        t.nop().then_inc(BDN.h, 1)
                        t.wait_ge(BSL.h, bsl0 + ct + 1)
                        for qi in range(4):
                            if 4 * ct + qi >= 1:
                                t.wait_ge(BTE.h, bte0 + 4 * ct + qi)
                            t.transpose(PTR2[:, :],
                                        GSD[:, qi * 128:(qi + 1) * 128],
                                        IDENT[:, :])
                            t.nop().then_inc(BTR.h, 1)
                BDN.n = bdn0 + NCT
                BTR.n = btr0 + NCT * 4

                @blk.vector
                def _(v):
                    v = W(v, SM[20])
                    for ct in range(NCT):
                        for qi in range(4):
                            v.wait_ge(BTR.h, btr0 + 4 * ct + qi + 1)
                            v.tensor_copy(GOB[:, qi * DM + ct * 128:
                                              qi * DM + (ct + 1) * 128],
                                          PTR2[:, :]).then_inc(BTE.h, 1)
                    for qi in range(4):
                        qt = hf * 4 + qi
                        v.wait_ge(BTE.h, bte0 + NCT * 4)
                        v.wait_ge(SL[14 + qi % 2].h,
                                  xl_sl0[qi % 2] + 16 * (qi // 2 + 1))
                        v.tensor_mul(T2B[:, :], SSB[:, qt * DM:(qt + 1) * DM],
                                     GOB[:, qi * DM:(qi + 1) * DM])
                        v.tensor_add(XOB[:, (qi % 2) * DM:(qi % 2 + 1) * DM],
                                     XOB[:, (qi % 2) * DM:(qi % 2 + 1) * DM],
                                     T2B[:, :]).then_inc(DNC.h, 1)
                BTE.n = bte0 + NCT * 4
                DNC.n = dnc0 + 4

        fstk.close()
        late.close()

    return nc, DH



def _stage_inputs(inputs, cfg):
    S, DM, ITR = cfg["S"], cfg["DM"], cfg["ITR"]
    R, PR, LNR = S // 8, S // 4, S // 2
    NST = S // 128
    T1 = LNR // 128
    NIT, NCT = ITR // 128, DM // 128

    x = np.ascontiguousarray(
        np.asarray(inputs["L0_attn_hidden_states"]).reshape(S, DM))
    Wq = np.asarray(inputs["L0_attn_Wq"])
    Wk = np.asarray(inputs["L0_attn_Wk"])
    Wv = np.asarray(inputs["L0_attn_Wv"])
    Wo = np.asarray(inputs["L0_attn_Wo"])
    pos = np.asarray(inputs["L0_attn_splat_positions"])
    dirs = np.asarray(inputs["L0_attn_splat_directions"])
    lsc = np.asarray(inputs["L0_attn_splat_log_scales"])
    lam = np.asarray(inputs["L0_attn_splat_log_amplitudes"])
    dstr = np.float32(inputs["L0_attn_directional_strength"])
    gstr = np.float32(inputs["L0_attn_gsa_strength"])
    wg = np.asarray(inputs["L0_ffn_w_gate"])
    wu = np.asarray(inputs["L0_ffn_w_up"])
    wd = np.asarray(inputs["L0_ffn_w_down"])
    sg = np.asarray(inputs["L0_ffn_s_gate"])
    su = np.asarray(inputs["L0_ffn_s_up"])
    sd = np.asarray(inputs["L0_ffn_s_down"])

    magic = np.random.randint(1, 2**31 - 1, size=(1, 4)).astype(np.int32)
    ident_b = np.eye(128, dtype=ml_dtypes.bfloat16)
    ident_f = np.eye(128, dtype=np.float32)
    ones_b = np.ones((128, 128), dtype=ml_dtypes.bfloat16)
    ones_f = np.ones((1, 128), dtype=np.float32)
    sg2 = np.ascontiguousarray(sg.reshape(NIT, 128).T.astype(np.float32))
    su2 = np.ascontiguousarray(su.reshape(NIT, 128).T.astype(np.float32))
    sd2 = np.ascontiguousarray(sd.reshape(NCT, 128).T.astype(np.float32))

    NHLoc = 8
    in_maps = []
    for c in range(NCORES):
        p = c % 2
        sibc = c ^ 1
        own_t0 = c * (R // 128)
        sib_t0 = sibc * (R // 128)
        x_ln = x[p * LNR:(p + 1) * LNR]
        x_own = x[c * R:(c + 1) * R]
        hsl = slice(p * (NHLoc * HD), (p + 1) * (NHLoc * HD))
        osl = slice((1 - p) * (NHLoc * HD), (2 - p) * (NHLoc * HD))
        wo_perm = np.concatenate([Wo[hsl, :], Wo[osl, :]], 0)
        hh = slice(p * NHLoc, (p + 1) * NHLoc)
        in_maps.append({
            "x_ln": np.ascontiguousarray(x_ln),
            "x_own": np.ascontiguousarray(x_own),
            "wq": np.ascontiguousarray(Wq[:, hsl]),
            "wk": np.ascontiguousarray(Wk[:, hsl]),
            "wv": np.ascontiguousarray(Wv[:, hsl]),
            "wo": np.ascontiguousarray(wo_perm),
            "spl_pos": np.ascontiguousarray(pos[hh].reshape(NHLoc * NS, HD)),
            "spl_dir": np.ascontiguousarray(dirs[hh].reshape(NHLoc * NS, HD)),
            "spl_ls64": np.ascontiguousarray(
                lsc[hh].reshape(NHLoc * NS, 1).astype(np.float32)),
            "spl_la": np.ascontiguousarray(lam[hh].astype(np.float32)),
            "scal": np.array([[dstr, gstr]], np.float32),
            "wgate": wg, "wup": wu, "wdown": wd,
            "sgate": sg2, "sup": su2, "sdown": sd2,
            "ident_b": ident_b, "ident_f": ident_f,
            "ones_b": ones_b, "ones_f": ones_f,
            "magic": magic,
            "idx": np.array([[p, 1 - p, own_t0 * 128, sib_t0 * 128,
                              p * LNR, 0, 0, 0]], np.int32),
            "fcol": np.concatenate(
                [np.full((128, 1), 1e-5, np.float32),
                 np.full((128, 1), -LNSQD, np.float32)], 1),
        })
    return in_maps


def kernel(**inputs):
    cfg = {"S": 8192, "DM": 2048, "ITR": 8192}
    in_maps = _stage_inputs(inputs, cfg)
    nc, _ = build_program(cfg)
    res = run_bass_kernel_spmd(nc, in_maps, core_ids=list(range(NCORES)))
    out = np.concatenate([res.results[c]["out"] for c in range(NCORES)], 0)
    return out.reshape(1, cfg["S"], cfg["DM"]).astype(np.float32)
